# version 01 (final)
# baseline (speedup 1.0000x reference)
import numpy as np
import ml_dtypes
from contextlib import ExitStack
import concourse.bass as bass
import concourse.mybir as mybir
from concourse.bass_utils import run_bass_kernel_spmd

F32 = mybir.dt.float32
BF16 = mybir.dt.bfloat16
I32 = mybir.dt.int32
AF = mybir.ActivationFunctionType
ALU = mybir.AluOpType
AX = mybir.AxisListType
BFNP = ml_dtypes.bfloat16

NCORE = 8
SEQ = 8192
D = 2048
TPC = 1024
NTT = TPC // 128
NDC = D // 128
F_FF = 1408
NFC = F_FF // 128
EPS = 1e-5
GATH2GLOB = []
for _r in range(NCORE):
    GATH2GLOB += [_r, 15 - _r]



def shard_rows(x):
    return [np.ascontiguousarray(np.concatenate(
        [x[512 * c:512 * (c + 1)], x[512 * (15 - c):512 * (16 - c)]], 0))
        for c in range(NCORE)]


def unshard_rows(parts):
    out = np.empty((SEQ,) + parts[0].shape[1:], parts[0].dtype)
    for c in range(NCORE):
        out[512 * c:512 * (c + 1)] = parts[c][:512]
        out[512 * (15 - c):512 * (16 - c)] = parts[c][512:]
    return out



ENGS = ("sync", "vector", "scalar", "tensor", "gpsimd")


class Prog:

    def __init__(self):
        self.nc = bass.Bass("TRN2", target_bir_lowering=False)
        self.es = ExitStack()
        self.cnt = {}
        self.sems = {}
        self.queues = {e: [] for e in ENGS}
        for s_ in ("ld", "vs", "ss", "ps", "gs", "st"):
            self.sem(s_)

    def sb(self, name, shape, dtype):
        return self.es.enter_context(self.nc.sbuf_tensor(name, shape, dtype))

    def ps(self, name, shape, dtype=F32):
        return self.es.enter_context(self.nc.psum_tensor(name, shape, dtype))

    def sem(self, name):
        h = self.es.enter_context(self.nc.semaphore(name))
        self.sems[name] = h
        self.cnt[name] = 0
        return h

    def on(self, eng, fn, waits=(), inc=None, dma=False):
        delta = 16 if dma else 1
        tok = None
        if inc is not None:
            if inc not in self.cnt:
                self.sem(inc)
            self.cnt[inc] += delta
            tok = (inc, self.cnt[inc])
        wl = tuple(w for w in waits if w is not None)
        sems = self.sems

        def play(e, fn=fn, wl=wl, inc=inc, delta=delta):
            for (s, v) in wl:
                e.wait_ge(sems[s], v)
            ins = fn(e)
            if inc is not None:
                ins.then_inc(sems[inc], delta)
            return ins
        self.queues[eng].append(play)
        return tok

    def barrier_wait(self, eng, *tokens):
        sems = self.sems
        toks = tuple(t for t in tokens if t is not None)

        def play(e):
            for (s, v) in toks:
                e.wait_ge(sems[s], v)
        self.queues[eng].append(play)

    def tok(self, name):
        return (name, self.cnt[name])

    def build(self):
        with self.nc.Block() as block:
            for eng in ENGS:
                if not self.queues[eng]:
                    continue
                dec = getattr(block, eng)

                def body(e, q=tuple(self.queues[eng])):
                    for f in q:
                        f(e)
                dec(body)
        return self.nc



class LNCtx:
    def __init__(self, p, tag="", scr=None):
        self.scr = scr if scr is not None else p.sb(f"ln_scr{tag}", [128, D], F32)
        self.ssum = p.sb(f"ln_ssum{tag}", [128, 1], F32)
        self.ssq = p.sb(f"ln_ssq{tag}", [128, 1], F32)
        self.negm = p.sb(f"ln_negm{tag}", [128, 1], F32)
        self.msq = p.sb(f"ln_msq{tag}", [128, 1], F32)
        self.var = p.sb(f"ln_var{tag}", [128, 1], F32)
        self.sdv = p.sb(f"ln_sdv{tag}", [128, 1], F32)
        self.rstd = p.sb(f"ln_rstd{tag}", [128, 1], F32)
        self.epsb = p.sb(f"ln_eps{tag}", [128, 1], F32)
        self.eps_tok = p.on("gpsimd", lambda g: g.memset(self.epsb[:], EPS), inc="gs")
        self.last_var_tok = None


def emit_ln(p, ln, x_ap, y_ap, x_ready, y_war=(), sq_war=()):
    t_sum = p.on("vector", lambda v: v.reduce_sum(out=ln.ssum[:], in_=x_ap, axis=AX.X),
                 waits=[x_ready], inc="vs")
    t_sq = p.on("scalar", lambda s: s.activation(out=ln.scr[:], in_=x_ap, func=AF.Square,
                                                 accum_out=ln.ssq[:]),
                waits=[x_ready, ln.last_var_tok] + list(sq_war), inc="ss")
    t_m = p.on("vector", lambda v: v.tensor_scalar_mul(ln.negm[:], ln.ssum[:], -1.0 / D),
               waits=[t_sum], inc="vs")
    t_msq = p.on("vector", lambda v: v.tensor_scalar(out=ln.msq[:], in0=ln.negm[:],
                                                     scalar1=ln.negm[:, 0:1], scalar2=None,
                                                     op0=ALU.mult),
                 waits=[t_m], inc="vs")
    t_var = p.on("vector", lambda v: v.scalar_tensor_tensor(out=ln.var[:], in0=ln.ssq[:],
                                                            scalar=1.0 / D, in1=ln.msq[:],
                                                            op0=ALU.mult, op1=ALU.subtract),
                 waits=[t_sq, t_msq], inc="vs")
    ln.last_var_tok = t_var
    t_sd = p.on("scalar", lambda s: s.activation(out=ln.sdv[:], in_=ln.var[:],
                                                 func=AF.Sqrt, bias=ln.epsb[:, 0:1]),
                waits=[t_var, ln.eps_tok], inc="ss")
    t_r = p.on("vector", lambda v: v.reciprocal(out=ln.rstd[:], in_=ln.sdv[:]),
               waits=[t_sd], inc="vs")
    t_y = p.on("vector", lambda v: v.tensor_scalar(out=y_ap, in0=x_ap,
                                                   scalar1=ln.negm[:, 0:1],
                                                   scalar2=ln.rstd[:, 0:1],
                                                   op0=ALU.add, op1=ALU.mult),
               waits=[t_r] + list(y_war), inc="vs")
    return t_y


def emit_ln_transpose_phase(p, ln, xin, xt, ybf, yT, pyT, identb, t_id,
                            first_sq_war=None):
    pe_tr = [None] * NTT
    tr_done = [None] * NTT
    for tt in range(NTT):
        t_x = p.on("sync", lambda s, tt=tt: s.dma_start(
            out=xt[:], in_=xin[tt * 128:(tt + 1) * 128, :]),
            waits=[tr_done[tt - 1] if tt >= 1 else None], inc="dx", dma=True)
        t_y = emit_ln(p, ln, xt[:], ybf[:], t_x,
                      y_war=[pe_tr[tt - 1] if tt >= 1 else None],
                      sq_war=[first_sq_war if tt == 0 else None])

        def trs(t, tt=tt):
            for dc in range(NDC):
                ins = t.transpose(pyT[:, dc * 128:(dc + 1) * 128],
                                  ybf[:, dc * 128:(dc + 1) * 128], identb[:])
            return ins
        pe_tr[tt] = p.on("tensor", trs,
                         waits=[t_y, t_id, tr_done[tt - 1] if tt >= 1 else None],
                         inc="ps")

        def cp(v, tt=tt):
            return v.tensor_copy(
                out=yT[:].rearrange("p (c n) -> p c n", c=NDC)[:, :, tt * 128:(tt + 1) * 128],
                in_=pyT[:].rearrange("p (c n) -> p c n", c=NDC))
        tr_done[tt] = p.on("vector", cp, waits=[pe_tr[tt]], inc="vs")
    return tr_done[-1]


def emit_identity(p, ident):
    p.on("gpsimd", lambda g: g.memset(ident[:], 0.0))
    return p.on("gpsimd", lambda g: g.affine_select(
        out=ident[:], in_=ident[:], compare_op=ALU.not_equal, fill=1.0, base=0,
        pattern=[[-1, ident.shape[0]]], channel_multiplier=1), inc="gs")



def build_gate_sums():
    p = Prog()
    nc = p.nc
    xin = nc.dram_tensor("x", [TPC, D], F32, kind="ExternalInput").ap()
    out = nc.dram_tensor("psums", [1, D], F32, kind="ExternalOutput").ap()

    xt = [p.sb(f"xt{i}", [128, D], F32) for i in range(2)]
    yt = [p.sb(f"yt{i}", [128, D], F32) for i in range(2)]
    ones = p.sb("ones", [128, 1], F32)
    sums_sb = p.sb("sums_sb", [1, D], F32)
    pp = p.ps("pp", [1, D], F32)
    ln = LNCtx(p)

    t_ones = p.on("gpsimd", lambda g: g.memset(ones[:], 1.0), inc="gs")
    pe_done = [None] * NTT
    for tt in range(NTT):
        b = tt % 2
        t_x = p.on("sync", lambda s, tt=tt, b=b: s.dma_start(
            out=xt[b][:], in_=xin[tt * 128:(tt + 1) * 128, :]),
            waits=[pe_done[tt - 2] if tt >= 2 else None], inc=f"dx{b}", dma=True)
        t_y = emit_ln(p, ln, xt[b][:], yt[b][:], t_x,
                      y_war=[pe_done[tt - 2] if tt >= 2 else None])

        def mms(t, tt=tt, b=b):
            for dc in range(4):
                ins = t.matmul(pp[0:1, dc * 512:(dc + 1) * 512], ones[:],
                               yt[b][:, dc * 512:(dc + 1) * 512],
                               start=(tt == 0), stop=(tt == NTT - 1))
            return ins
        pe_done[tt] = p.on("tensor", mms, waits=[t_y, t_ones], inc="ps")

    t_ev = p.on("vector", lambda v: v.tensor_copy(out=sums_sb[:], in_=pp[0:1, :]),
                waits=[pe_done[NTT - 1]], inc="vs")
    t_st = p.on("gpsimd", lambda g: g.dma_start(out=out, in_=sums_sb[:]),
                waits=[t_ev], inc="st", dma=True)
    p.barrier_wait("gpsimd", t_st)
    return p



def build_gate_coefs():
    p = Prog()
    nc = p.nc
    partsT = nc.dram_tensor("partsT", [D, NCORE], F32, kind="ExternalInput").ap()
    gateWT = nc.dram_tensor("gateWT", [D, 6], F32, kind="ExternalInput").ap()
    c6out = nc.dram_tensor("c6", [1, 6], F32, kind="ExternalOutput").ap()

    ptile = p.sb("ptile", [128, 16 * NCORE], F32)
    gwt = p.sb("gwt", [128, 16 * 6], F32)
    meanT = p.sb("meanT", [128, 16], F32)
    lg = p.sb("lg", [1, 6], F32)
    sc1 = {n: p.sb(n, [1, 1], F32) for n in
           ("mx", "negmx", "se", "rse", "m1", "m2", "m12", "rm12")}
    r6 = {n: p.sb(n, [1, 6], F32) for n in
          ("pr", "probs", "msk1", "pd", "sel", "cc", "c6sb")}
    pl = p.ps("pl", [1, 6], F32)

    t1 = p.on("sync", lambda s: s.dma_start(out=ptile[:], in_=partsT.rearrange(
        "(c p) r -> p c r", p=128)), inc="d1", dma=True)
    t2 = p.on("sync", lambda s: s.dma_start(out=gwt[:], in_=gateWT.rearrange(
        "(c p) e -> p c e", p=128)), inc="d2", dma=True)
    t3 = p.on("vector", lambda v: v.reduce_sum(
        out=meanT[:], in_=ptile[:].rearrange("p (c r) -> p c r", c=16), axis=AX.X),
        waits=[t1], inc="vs")

    def mms(t):
        for c in range(16):
            ins = t.matmul(pl[:], meanT[:, c:c + 1], gwt[:, c * 6:(c + 1) * 6],
                           start=(c == 0), stop=(c == 15))
        return ins
    t4 = p.on("tensor", mms, waits=[t2, t3], inc="ps")
    t = p.on("scalar", lambda s: s.activation(out=lg[:], in_=pl[:], func=AF.Copy,
                                              scale=1.0 / SEQ), waits=[t4], inc="ss")
    t = p.on("vector", lambda v: v.reduce_max(out=sc1["mx"][:], in_=lg[:], axis=AX.X),
             waits=[t], inc="vs")
    t = p.on("vector", lambda v: v.tensor_scalar_mul(sc1["negmx"][:], sc1["mx"][:], -1.0),
             waits=[t], inc="vs")
    t = p.on("scalar", lambda s: s.activation(out=r6["pr"][:], in_=lg[:], func=AF.Exp,
                                              bias=sc1["negmx"][:, 0:1],
                                              accum_out=sc1["se"][:]),
             waits=[t], inc="ss")
    t = p.on("vector", lambda v: v.reciprocal(out=sc1["rse"][:], in_=sc1["se"][:]),
             waits=[t], inc="vs")
    t = p.on("vector", lambda v: v.tensor_scalar_mul(r6["probs"][:], r6["pr"][:],
                                                     sc1["rse"][:, 0:1]), waits=[t], inc="vs")
    t = p.on("vector", lambda v: v.reduce_max(out=sc1["m1"][:], in_=r6["probs"][:],
                                              axis=AX.X), waits=[t], inc="vs")
    t = p.on("vector", lambda v: v.tensor_scalar(out=r6["msk1"][:], in0=r6["probs"][:],
                                                 scalar1=sc1["m1"][:, 0:1], scalar2=None,
                                                 op0=ALU.is_ge), waits=[t], inc="vs")
    t = p.on("vector", lambda v: v.scalar_tensor_tensor(out=r6["pd"][:], in0=r6["msk1"][:],
                                                        scalar=-2.0, in1=r6["probs"][:],
                                                        op0=ALU.mult, op1=ALU.add),
             waits=[t], inc="vs")
    t = p.on("vector", lambda v: v.reduce_max(out=sc1["m2"][:], in_=r6["pd"][:], axis=AX.X),
             waits=[t], inc="vs")
    t = p.on("vector", lambda v: v.tensor_scalar(out=r6["sel"][:], in0=r6["probs"][:],
                                                 scalar1=sc1["m2"][:, 0:1], scalar2=None,
                                                 op0=ALU.is_ge), waits=[t], inc="vs")
    t = p.on("vector", lambda v: v.tensor_tensor(out=sc1["m12"][:], in0=sc1["m1"][:],
                                                 in1=sc1["m2"][:], op=ALU.add),
             waits=[t], inc="vs")
    t = p.on("vector", lambda v: v.reciprocal(out=sc1["rm12"][:], in_=sc1["m12"][:]),
             waits=[t], inc="vs")
    t = p.on("vector", lambda v: v.tensor_tensor(out=r6["cc"][:], in0=r6["sel"][:],
                                                 in1=r6["probs"][:], op=ALU.mult),
             waits=[t], inc="vs")
    t = p.on("vector", lambda v: v.tensor_scalar_mul(r6["c6sb"][:], r6["cc"][:],
                                                     sc1["rm12"][:, 0:1]),
             waits=[t], inc="vs")
    t = p.on("gpsimd", lambda g: g.dma_start(out=c6out, in_=r6["c6sb"][:]),
             waits=[t], inc="st", dma=True)
    p.barrier_wait("gpsimd", t)
    return p



def build_moe_main(dbg=False):
    p = Prog()
    nc = p.nc
    xin = nc.dram_tensor("x", [TPC, D], F32, kind="ExternalInput").ap()
    c6p = nc.dram_tensor("c6p", [6, 1], F32, kind="ExternalInput").ap()
    c6r = nc.dram_tensor("c6r", [1, 6], F32, kind="ExternalInput").ap()
    W1 = nc.dram_tensor("W1", [6, D, F_FF], F32, kind="ExternalInput").ap()
    b1T = nc.dram_tensor("b1T", [F_FF, 6], F32, kind="ExternalInput").ap()
    W2 = nc.dram_tensor("W2", [6, F_FF, D], F32, kind="ExternalInput").ap()
    b2p = nc.dram_tensor("b2p", [6, D], F32, kind="ExternalInput").ap()
    xout = nc.dram_tensor("xo", [TPC, D], F32, kind="ExternalOutput").ap()
    if dbg:
        dbg_yT = nc.dram_tensor("dbg_yT", [128, NDC * TPC], BF16, kind="ExternalOutput").ap()
        dbg_hT = nc.dram_tensor("dbg_hT", [128, NFC * TPC], BF16, kind="ExternalOutput").ap()
        dbg_w2 = nc.dram_tensor("dbg_w2", [128, NFC * D], BF16, kind="ExternalOutput").ap()
        dbg_ea = nc.dram_tensor("dbg_ea", [128, NTT * D], F32, kind="ExternalOutput").ap()

    xt = p.sb("xt", [128, D], F32)
    ybf = p.sb("ybf", [128, D], BF16)
    yT = p.sb("yT", [128, NDC * TPC], BF16)
    identb = p.sb("identb", [128, 128], BF16)
    wf32 = [p.sb(f"wf32{i}", [128, 1024], F32) for i in range(2)]
    wbf = p.sb("wbf", [128, NDC * F_FF], BF16)
    hT = p.sb("hT", [128, NFC * TPC], BF16)
    hscr = [p.sb(f"hscr{i}", [128, 512], F32) for i in range(2)]
    eacc = p.sb("eacc", [128, NTT * D], F32)
    b1Ts = p.sb("b1Ts", [128, NFC * 6], F32)
    c6sb = p.sb("c6sb", [6, 1], F32)
    c6rsb = p.sb("c6rsb", [1, 6], F32)
    cbc = p.sb("cbc", [128, 6], F32)
    cb2 = p.sb("cb2", [1, D], F32)
    ones1 = p.sb("ones1", [1, 512], F32)
    xo_sb = p.sb("xo_sb", [128, D], F32)
    ln = LNCtx(p, scr=xo_sb)

    pyT = p.ps("pyT", [128, D], BF16)
    ph = [p.ps(f"ph{i}", [128, 512], F32) for i in range(2)]
    pe_ = [p.ps(f"pe{i}", [128, 512], F32) for i in range(2)]

    t_ones = p.on("gpsimd", lambda g: g.memset(ones1[:], 1.0), inc="gs")
    t_id = emit_identity(p, identb)
    p.on("sync", lambda s: s.dma_start(out=c6sb[:], in_=c6p), inc="dm", dma=True)
    p.on("sync", lambda s: s.dma_start(out=c6rsb[:], in_=c6r), inc="dm", dma=True)
    p.on("sync", lambda s: s.dma_start(
        out=b1Ts[:], in_=b1T.rearrange("(c p) e -> p c e", p=128)), inc="dm", dma=True)
    p.on("sync", lambda s: s.dma_start(out=wf32[0][0:6, :], in_=b2p[:, 0:1024]),
         inc="dm", dma=True)
    p.on("sync", lambda s: s.dma_start(out=wf32[1][0:6, :], in_=b2p[:, 1024:2048]),
         inc="dm", dma=True)
    t_misc = p.tok("dm")
    t_c6 = t_c6r = t_b1 = t_b2a = t_b2b = t_b2 = t_misc
    t_bc = p.on("tensor", lambda t: t.matmul(ph[0][:, 0:6], ones1[0:1, 0:128], c6rsb[:],
                                             start=True, stop=True),
                waits=[t_c6r, t_ones], inc="ps")
    t_cbc = p.on("vector", lambda v: v.tensor_copy(out=cbc[:], in_=ph[0][:, 0:6]),
                 waits=[t_bc], inc="vs")
    prev = [t_b2a, t_b2b, t_c6, t_cbc]
    for dc in range(4):
        buf = wf32[dc // 2]
        t_mm = p.on("tensor", lambda t, dc=dc, buf=buf: t.matmul(
            ph[0][0:1, :], c6sb[:], buf[0:6, (dc % 2) * 512:(dc % 2 + 1) * 512],
            start=True, stop=True), waits=prev, inc="ps")
        t_cp = p.on("vector", lambda v, dc=dc: v.tensor_copy(
            out=cb2[0:1, dc * 512:(dc + 1) * 512], in_=ph[0][0:1, :]),
            waits=[t_mm], inc="vs")
        prev = [t_cp]
    t_cb2 = prev[0]

    yT_ready = emit_ln_transpose_phase(p, ln, xin, xt, ybf, yT, pyT, identb, t_id,
                                       first_sq_war=t_cb2)

    h_evac_done = [None, None]
    eo_evac_done = [None, None]
    last_mm_on_wbf = None
    last_eo_mm = None
    prev_casts = [None, None]
    acc_tok = {}
    for e in range(6):
        cast_toks = []
        i = 0
        for dc in range(NDC):
            for hf in range(2):
                bb = i % 2
                t_w = p.on("sync", lambda s, e=e, dc=dc, hf=hf, bb=bb: s.dma_start(
                    out=wf32[bb][:, 0:704],
                    in_=W1[e, dc * 128:(dc + 1) * 128, hf * 704:(hf + 1) * 704]),
                    waits=[cast_toks[i - 2] if i >= 2 else prev_casts[bb],
                           t_cb2 if (e == 0 and i <= 1) else None],
                    inc=f"dw{bb}", dma=True)
                t_c = p.on("vector", lambda v, dc=dc, hf=hf, bb=bb: v.tensor_copy(
                    out=wbf[:, dc * F_FF + hf * 704:dc * F_FF + (hf + 1) * 704],
                    in_=wf32[bb][:, 0:704]),
                    waits=[t_w, last_mm_on_wbf if i == 0 else None], inc="vs")
                cast_toks.append(t_c)
                i += 1
        prev_casts = [cast_toks[i - 2], cast_toks[i - 1]]
        if (i - 2) % 2 != 0:
            prev_casts = [cast_toks[i - 1], cast_toks[i - 2]]
        w1_ready = cast_toks[-1]

        first_h = True
        for fc in range(NFC):
            for th in range(2):
                bank = (fc * 2 + th) % 2

                def mmh(t, e=e, fc=fc, th=th, bank=bank):
                    for dc in range(NDC):
                        ins = t.matmul(ph[bank][:],
                                       wbf[:, dc * F_FF + fc * 128:dc * F_FF + (fc + 1) * 128],
                                       yT[:, dc * TPC + th * 512:dc * TPC + (th + 1) * 512],
                                       start=(dc == 0), stop=(dc == NDC - 1))
                    return ins
                mm_tok = p.on("tensor", mmh,
                              waits=[w1_ready, yT_ready, h_evac_done[bank]], inc="ps")
                t_g = p.on("scalar", lambda s, e=e, fc=fc, bank=bank: s.activation(
                    out=hscr[bank][:], in_=ph[bank][:], func=AF.Gelu,
                    bias=b1Ts[:, fc * 6 + e:fc * 6 + e + 1]),
                    waits=[mm_tok, t_b1, h_evac_done[bank]], inc="ss")
                t_h = p.on("vector", lambda v, e=e, fc=fc, th=th, bank=bank: v.tensor_scalar_mul(
                    hT[:, fc * TPC + th * 512:fc * TPC + (th + 1) * 512],
                    hscr[bank][:], cbc[:, e:e + 1]),
                    waits=[t_g, t_cbc, last_eo_mm if first_h else None], inc="vs")
                first_h = False
                h_evac_done[bank] = t_h
        hT_ready = [h_evac_done[0], h_evac_done[1]]
        last_mm_on_wbf = mm_tok
        if dbg and e == 0:
            t_d = p.on("gpsimd", lambda g: g.dma_start(out=dbg_yT, in_=yT[:]),
                       waits=[yT_ready], inc="st", dma=True)
            t_d2 = p.on("gpsimd", lambda g: g.dma_start(out=dbg_hT, in_=hT[:]),
                       waits=hT_ready, inc="st", dma=True)
            last_eo_mm = t_d2

        cast_toks = []
        i = 0
        for fc in range(NFC):
            for hf in range(2):
                bb = i % 2
                t_w = p.on("sync", lambda s, e=e, fc=fc, hf=hf, bb=bb: s.dma_start(
                    out=wf32[bb][:],
                    in_=W2[e, fc * 128:(fc + 1) * 128, hf * 1024:(hf + 1) * 1024]),
                    waits=[cast_toks[i - 2] if i >= 2 else prev_casts[bb]],
                    inc=f"dw{bb}", dma=True)
                t_c = p.on("vector", lambda v, fc=fc, hf=hf, bb=bb: v.tensor_copy(
                    out=wbf[:, fc * D + hf * 1024:fc * D + (hf + 1) * 1024],
                    in_=wf32[bb][:]),
                    waits=[t_w, last_mm_on_wbf if i == 0 else None], inc="vs")
                cast_toks.append(t_c)
                i += 1
        prev_casts = [cast_toks[i - 2], cast_toks[i - 1]]
        if (i - 2) % 2 != 0:
            prev_casts = [cast_toks[i - 1], cast_toks[i - 2]]
        w2_ready = cast_toks[-1]
        if dbg and e == 0:
            t_d3 = p.on("gpsimd", lambda g: g.dma_start(out=dbg_w2, in_=wbf[:]),
                        waits=[w2_ready], inc="st", dma=True)
            last_mm_on_wbf = t_d3

        for tt in range(NTT):
            for dc in range(4):
                bank = (tt * 4 + dc) % 2

                def mme(t, e=e, tt=tt, dc=dc, bank=bank):
                    first = True
                    if e == 0:
                        ins = t.matmul(pe_[bank][:], ones1[0:1, 0:128],
                                       cb2[0:1, dc * 512:(dc + 1) * 512],
                                       start=True, stop=False)
                        first = False
                    for fc in range(NFC):
                        ins = t.matmul(pe_[bank][:],
                                       hT[:, fc * TPC + tt * 128:fc * TPC + (tt + 1) * 128],
                                       wbf[:, fc * D + dc * 512:fc * D + (dc + 1) * 512],
                                       start=first, stop=(fc == NFC - 1))
                        first = False
                    return ins
                t_mm = p.on("tensor", mme,
                            waits=[w2_ready, t_cb2, t_ones, hT_ready[0], hT_ready[1],
                                   eo_evac_done[bank]], inc="ps")
                if e == 0:
                    t_ev = p.on("vector", lambda v, tt=tt, dc=dc, bank=bank: v.tensor_copy(
                        out=eacc[:, tt * D + dc * 512:tt * D + (dc + 1) * 512],
                        in_=pe_[bank][:]), waits=[t_mm], inc="vs")
                else:
                    t_ev = p.on("vector", lambda v, tt=tt, dc=dc, bank=bank: v.scalar_tensor_tensor(
                        out=eacc[:, tt * D + dc * 512:tt * D + (dc + 1) * 512],
                        in0=pe_[bank][:], scalar=1.0,
                        in1=eacc[:, tt * D + dc * 512:tt * D + (dc + 1) * 512],
                        op0=ALU.mult, op1=ALU.add),
                        waits=[t_mm, acc_tok.get((tt, dc))], inc="vs")
                acc_tok[(tt, dc)] = t_ev
                eo_evac_done[bank] = t_ev
        if dbg and e == 0:
            t_d4 = p.on("gpsimd", lambda g: g.dma_start(out=dbg_ea, in_=eacc[:]),
                        waits=[acc_tok[(tt2, dc2)] for tt2 in range(NTT) for dc2 in range(4)],
                        inc="st", dma=True)
            for tt2 in range(NTT):
                for dc2 in range(4):
                    acc_tok[(tt2, dc2)] = t_d4
        last_eo_mm = t_mm
        last_mm_on_wbf = t_mm

    st_prev = None
    for tt in range(NTT):
        t_x = p.on("sync", lambda s, tt=tt: s.dma_start(
            out=xt[:], in_=xin[tt * 128:(tt + 1) * 128, :]),
            waits=[st_prev], inc="dx", dma=True)
        t_o = p.on("vector", lambda v, tt=tt: v.scalar_tensor_tensor(
            out=xo_sb[:], in0=eacc[:, tt * D:(tt + 1) * D], scalar=0.5,
            in1=xt[:], op0=ALU.mult, op1=ALU.add),
            waits=[t_x, st_prev] + [acc_tok[(tt, dc)] for dc in range(4)], inc="vs")
        st_prev = p.on("gpsimd", lambda g, tt=tt: g.dma_start(
            out=xout[tt * 128:(tt + 1) * 128, :], in_=xo_sb[:]),
            waits=[t_o], inc="st", dma=True)
    p.barrier_wait("gpsimd", p.tok("st"))
    return p



def build_qkv(dq, dkv):
    nj_q = dq // 128
    nj_k = dkv // 128
    njc = nj_q + 2 * nj_k
    p = Prog()
    nc = p.nc
    xin = nc.dram_tensor("x", [TPC, D], F32, kind="ExternalInput").ap()
    Wf = nc.dram_tensor("Wf", [D, 128 * njc], F32, kind="ExternalInput").ap()
    bTl = nc.dram_tensor("bTl", [128, njc], F32, kind="ExternalInput").ap()
    QTo = nc.dram_tensor("QT", [dq, TPC], BF16, kind="ExternalOutput").ap()
    KTo = nc.dram_tensor("KT", [dkv, TPC], BF16, kind="ExternalOutput").ap()
    Vo = nc.dram_tensor("V", [TPC, dkv], BF16, kind="ExternalOutput").ap()
    qn2o = nc.dram_tensor("qn2", [1, nj_q], F32, kind="ExternalOutput").ap()
    kn2o = nc.dram_tensor("kn2", [1, nj_k], F32, kind="ExternalOutput").ap()

    xt = p.sb("xt", [128, D], F32)
    ybf = p.sb("ybf", [128, D], BF16)
    yT = p.sb("yT", [128, NDC * TPC], BF16)
    identb = p.sb("identb", [128, 128], BF16)
    wst = [p.sb(f"wst{i}", [128, D], F32) for i in range(2)]
    wcb = [p.sb(f"wcb{i}", [128, D], BF16) for i in range(2)]
    QT_sb = p.sb("QT_sb", [128, nj_q * TPC], BF16)
    KT_sb = p.sb("KT_sb", [128, nj_k * TPC], BF16)
    V_sb = p.sb("V_sb", [128, NTT * dkv], BF16)
    vstage = p.sb("vstage", [128, 512], BF16)
    bT_sb = p.sb("bT_sb", [128, njc], F32)
    sqs = [p.sb(f"sqs{i}", [128, 512], F32) for i in range(2)]
    n1 = p.sb("n1", [1, 1], F32)
    qn2_sb = p.sb("qn2_sb", [1, nj_q], F32)
    kn2_sb = p.sb("kn2_sb", [1, nj_k], F32)
    onesc = p.sb("onesc", [128, 1], F32)
    ln = LNCtx(p)

    pyT = p.ps("pyT", [128, D], BF16)
    pj = [p.ps(f"pj{i}", [128, 512], F32) for i in range(2)]
    pvt = [p.ps(f"pvt{i}", [128, 128], BF16) for i in range(2)]
    pn = p.ps("pn", [1, 512], F32)

    t_onesc = p.on("gpsimd", lambda g: g.memset(onesc[:], 1.0), inc="gs")
    p.on("gpsimd", lambda g: g.memset(qn2_sb[:], 0.0))
    t_z = p.on("gpsimd", lambda g: g.memset(kn2_sb[:], 0.0), inc="gs")
    t_id = emit_identity(p, identb)
    t_bT = p.on("sync", lambda s: s.dma_start(out=bT_sb[:], in_=bTl), inc="dm", dma=True)

    yT_ready = emit_ln_transpose_phase(p, ln, xin, xt, ybf, yT, pyT, identb, t_id)

    cast_prev = [None, None]
    mm_prev = [None, None]
    evac_done = [None, None]
    vtr_done = [None, None]
    st_toks = []
    nacc_prev = None
    nmax_prev = {"q": None, "k": None}
    for jc in range(njc):
        bb = jc % 2
        t_w = p.on("sync", lambda s, jc=jc, bb=bb: s.dma_start(
            out=wst[bb][:],
            in_=Wf[:, jc * 128:(jc + 1) * 128].rearrange("(c p) j -> p c j", p=128)),
            waits=[cast_prev[bb]], inc=f"dw{bb}", dma=True)
        t_c = p.on("vector", lambda v, bb=bb: v.tensor_copy(out=wcb[bb][:], in_=wst[bb][:]),
                   waits=[t_w, mm_prev[bb]], inc="vs")
        cast_prev[bb] = t_c
        kind = "q" if jc < nj_q else ("k" if jc < nj_q + nj_k else "v")
        for th in range(2):
            bank = (jc * 2 + th) % 2

            def mmj(t, jc=jc, th=th, bank=bank, bb=bb):
                for dc in range(NDC):
                    ins = t.matmul(pj[bank][:],
                                   wcb[bb][:, dc * 128:(dc + 1) * 128],
                                   yT[:, dc * TPC + th * 512:dc * TPC + (th + 1) * 512],
                                   start=(dc == 0), stop=(dc == NDC - 1))
                return ins
            mm = p.on("tensor", mmj, waits=[t_c, yT_ready, evac_done[bank]], inc="ps")
            mm_prev[bb] = mm

            if kind == "v":
                vj = jc - nj_q - nj_k
                t_e = p.on("vector", lambda v, jc=jc, bank=bank: v.tensor_scalar(
                    out=vstage[:], in0=pj[bank][:], scalar1=bT_sb[:, jc:jc + 1],
                    scalar2=None, op0=ALU.add),
                    waits=[mm, t_bT, vtr_done[0], vtr_done[1]], inc="vs")
                evac_done[bank] = t_e
                for u in range(4):
                    tb = (th * 4 + u) % 2
                    t_tr = p.on("tensor", lambda t, u=u, tb=tb: t.transpose(
                        pvt[tb][:], vstage[:, u * 128:(u + 1) * 128], identb[:]),
                        waits=[t_e, vtr_done[tb]], inc="ps")
                    t_cv = p.on("vector", lambda v, th=th, u=u, vj=vj, tb=tb: v.tensor_copy(
                        out=V_sb[:, (th * 4 + u) * dkv + vj * 128:(th * 4 + u) * dkv + (vj + 1) * 128],
                        in_=pvt[tb][:]), waits=[t_tr], inc="vs")
                    vtr_done[tb] = t_cv
            else:
                dst = QT_sb if kind == "q" else KT_sb
                col = jc if kind == "q" else jc - nj_q
                t_e = p.on("vector", lambda v, jc=jc, bank=bank, dst=dst, col=col, th=th: v.tensor_scalar(
                    out=dst[:, col * TPC + th * 512:col * TPC + (th + 1) * 512],
                    in0=pj[bank][:], scalar1=bT_sb[:, jc:jc + 1],
                    scalar2=None, op0=ALU.add),
                    waits=[mm, t_bT], inc="vs")
                t_sq = p.on("scalar", lambda s, bank=bank: s.activation(
                    out=sqs[bank][:], in_=pj[bank][:], func=AF.Square),
                    waits=[t_e, nacc_prev], inc="ss")
                evac_done[bank] = t_sq
                t_nm = p.on("tensor", lambda t, bank=bank: t.matmul(
                    pn[:], onesc[:], sqs[bank][:], start=True, stop=True),
                    waits=[t_sq, t_onesc, nacc_prev], inc="ps")
                t_mx = p.on("vector", lambda v: v.reduce_max(out=n1[:], in_=pn[:],
                                                             axis=AX.X),
                            waits=[t_nm], inc="vs")
                nsb = qn2_sb if kind == "q" else kn2_sb
                t_acc = p.on("vector", lambda v, nsb=nsb, col=col: v.tensor_tensor(
                    out=nsb[0:1, col:col + 1], in0=nsb[0:1, col:col + 1],
                    in1=n1[:], op=ALU.max),
                    waits=[t_mx, t_z, nmax_prev[kind]], inc="vs")
                nmax_prev[kind] = t_acc
                nacc_prev = t_acc

    t = p.on("gpsimd", lambda g: g.dma_start(
        out=QTo.rearrange("(c p) t -> p c t", p=128), in_=QT_sb[:]),
        waits=[evac_done[0], evac_done[1], p.tok("ss")], inc="st", dma=True)
    st_toks.append(t)
    t = p.on("gpsimd", lambda g: g.dma_start(
        out=KTo.rearrange("(c p) t -> p c t", p=128), in_=KT_sb[:]),
        waits=[p.tok("ss")], inc="st", dma=True)
    st_toks.append(t)
    t = p.on("gpsimd", lambda g: g.dma_start(
        out=Vo.rearrange("(tt p) j -> p tt j", p=128), in_=V_sb[:]),
        waits=[vtr_done[0], vtr_done[1]], inc="st", dma=True)
    st_toks.append(t)
    t = p.on("gpsimd", lambda g: g.dma_start(out=qn2o, in_=qn2_sb[:]),
             waits=[nmax_prev["q"]], inc="st", dma=True)
    st_toks.append(t)
    t = p.on("gpsimd", lambda g: g.dma_start(out=kn2o, in_=kn2_sb[:]),
             waits=[nmax_prev["k"]], inc="st", dma=True)
    st_toks.append(t)
    p.barrier_wait("gpsimd", p.tok("st"))
    return p



def build_attn(causal, n_kv, dbg=False, ser_op=False):
    NH = 16
    dkv = n_kv * 128
    grp = NH // n_kv
    p = Prog()
    nc = p.nc
    xin = nc.dram_tensor("x", [TPC, D], F32, kind="ExternalInput").ap()
    QTd = nc.dram_tensor("QT", [D, TPC], BF16, kind="ExternalInput").ap()
    KTg = nc.dram_tensor("KTg", [dkv, SEQ], BF16, kind="ExternalInput").ap()
    Vg = nc.dram_tensor("Vg", [SEQ, dkv], BF16, kind="ExternalInput").ap()
    qn2 = nc.dram_tensor("qn2", [1, NH], F32, kind="ExternalInput").ap()
    kn2f = nc.dram_tensor("kn2f", [1, NCORE * NH], F32, kind="ExternalInput").ap()
    Wo = nc.dram_tensor("Wo", [D, D], F32, kind="ExternalInput").ap()
    bor = nc.dram_tensor("bor", [1, D], F32, kind="ExternalInput").ap()
    if causal:
        maskb = nc.dram_tensor("maskb", [128, 128], F32, kind="ExternalInput").ap()
    xout = nc.dram_tensor("xo", [TPC, D], F32, kind="ExternalOutput").ap()
    if dbg:
        dbg_at = nc.dram_tensor("dbg_at", [128, NH * TPC], BF16, kind="ExternalOutput").ap()
        dbg_nm = nc.dram_tensor("dbg_nm", [128, NH], F32, kind="ExternalOutput").ap()
        dbg_wo = nc.dram_tensor("dbg_wo", [128, NDC * 512], BF16, kind="ExternalOutput").ap()

    QT_sb = p.sb("QT_sb", [128, NH * TPC], BF16)
    ktile = [p.sb(f"ktile{i}", [128, SEQ], BF16) for i in range(2)]
    vtile = [p.sb(f"vtile{i}", [128, 64 * 128], BF16) for i in range(2)]
    PT = [p.sb(f"PT{i}", [128, 1024], BF16) for i in range(2)]
    attnT = p.sb("attnT", [128, NH * TPC], BF16)
    rsum1 = p.sb("rsum1", [1, TPC], F32)
    rsumb = p.sb("rsumb", [128, TPC], F32)
    qn2_sb = p.sb("qn2_sb", [1, NH], F32)
    kn2_sb = p.sb("kn2_sb", [1, NCORE * NH], F32)
    knmax = p.sb("knmax", [1, NH], F32)
    qrow = p.sb("qrow", [1, NH], F32)
    krow = p.sb("krow", [1, NH], F32)
    negM = p.sb("negM", [1, NH], F32)
    negMb = p.sb("negMb", [128, NH], F32)
    cmb = None
    if causal:
        maskb_sb = p.sb("maskb_sb", [128, 128], F32)
        cmb = p.sb("cmb", [128, 128 * NH], F32)
    xt = p.sb("xt", [128, D], F32)
    xo_sb = p.sb("xo_sb", [128, D], F32)
    wost = [p.sb(f"wost{i}", [128, 1024], F32) for i in range(2)]
    wocol = p.sb("wocol", [128, NDC * 512], BF16)
    borbf = p.sb("borbf", [1, D], BF16)
    onesrow = p.sb("onesrow", [1, 128], BF16)
    ones1 = p.sb("ones1", [1, 512], F32)
    onesb = p.sb("onesb", [128, 1], BF16)

    pA = p.ps("pA", [128, 1024], F32)
    pS = [p.ps(f"pS{i}", [128, 512], F32) for i in range(2)]
    psums = p.ps("psums", [1, 1024], F32)
    pO = [p.ps(f"pO{i}", [128, 512], F32) for i in range(2)]

    t_ones = p.on("gpsimd", lambda g: g.memset(ones1[:], 1.0), inc="gs")
    t_onesb = p.on("gpsimd", lambda g: g.memset(onesb[:], 1.0), inc="gs")
    t_onesr = p.on("gpsimd", lambda g: g.memset(onesrow[:], 1.0), inc="gs")
    p.on("sync", lambda s: s.dma_start(
        out=QT_sb[:], in_=QTd.rearrange("(c p) t -> p c t", p=128)), inc="dm", dma=True)
    p.on("sync", lambda s: s.dma_start(out=qn2_sb[:], in_=qn2), inc="dm", dma=True)
    p.on("sync", lambda s: s.dma_start(out=kn2_sb[:], in_=kn2f), inc="dm", dma=True)
    p.on("sync", lambda s: s.dma_start(out=xt[0:1, :], in_=bor), inc="dm", dma=True)
    if causal:
        p.on("sync", lambda s: s.dma_start(out=maskb_sb[:], in_=maskb),
             inc="dm", dma=True)
    t_misc = p.tok("dm")
    t_qt = t_qn = t_kn = t_bo = t_mb = t_misc
    t_km = p.on("vector", lambda v: v.reduce_max(
        out=knmax[:], in_=kn2_sb[0:1, :].rearrange("o (r h) -> o h r", r=NCORE),
        axis=AX.X), waits=[t_kn], inc="vs")
    t_sq1 = p.on("scalar", lambda s: s.activation(out=qrow[:], in_=qn2_sb[:],
                                                  func=AF.Sqrt), waits=[t_qn], inc="ss")
    t_sk1 = p.on("scalar", lambda s: s.activation(out=krow[:], in_=knmax[:],
                                                  func=AF.Sqrt), waits=[t_km], inc="ss")
    t_nm = p.on("vector", lambda v: v.tensor_tensor(
        out=negM[:], in0=qrow[:], in1=krow[:], op=ALU.mult),
        waits=[t_sq1, t_sk1], inc="vs")
    t_ng = p.on("vector", lambda v: v.tensor_scalar_mul(
        negM[:], negM[:], -1.0 / float(np.sqrt(128.0))), waits=[t_nm], inc="vs")
    t_nb = p.on("tensor", lambda t: t.matmul(pS[0][:, 0:NH], ones1[0:1, 0:128], negM[:],
                                             start=True, stop=True),
                waits=[t_ng, t_ones], inc="ps")
    t_nbe = p.on("vector", lambda v: v.tensor_copy(out=negMb[:], in_=pS[0][:, 0:NH]),
                 waits=[t_nb], inc="vs")
    if causal:
        cmb_toks = []
        tprev = None
        for col in range(128):
            tprev = p.on("vector", lambda v, col=col: v.tensor_scalar(
                out=cmb[:, col * NH:(col + 1) * NH], in0=negMb[:],
                scalar1=maskb_sb[:, col:col + 1], scalar2=None, op0=ALU.add),
                waits=[t_nbe, t_mb], inc="vs")
        t_cmb = tprev
    t_bob = p.on("vector", lambda v: v.tensor_copy(out=borbf[:], in_=xt[0:1, :]),
                 waits=[t_bo], inc="vs")

    kv_loaded = {}
    ev_attnT = None
    ev_rs = None
    pS_user = [t_nbe, t_bob]
    PT_user = [None, None]
    pA_user = None
    psums_user = None
    kv_tok = [None, None]
    kload_prev = [None, None]
    for h in range(NH):
        kv = h // grp
        kb = kv % 2
        if kv not in kv_loaded:
            t_k = p.on("sync", lambda s, kv=kv, kb=kb: s.dma_start(
                out=ktile[kb][:], in_=KTg[kv * 128:(kv + 1) * 128, :]),
                waits=[kload_prev[kb]], inc=f"dk{kb}", dma=True)
            t_v = p.on("sync", lambda s, kv=kv, kb=kb: s.dma_start(
                out=vtile[kb][:], in_=Vg[:, kv * 128:(kv + 1) * 128]
                .rearrange("(kc p) d -> p kc d", p=128)),
                waits=[kload_prev[kb]], inc=f"dv{kb}", dma=True)
            kv_loaded[kv] = (t_k, t_v)
        t_k, t_v = kv_loaded[kv]

        for qch in range(2):
            for kc in range(64):
                sb_ = (kc) % 2
                pb = (kc) % 2

                def mms_(t, h=h, qch=qch, kc=kc, sb_=sb_, kb=kb):
                    return t.matmul(pS[sb_][:],
                                    ktile[kb][:, kc * 128:(kc + 1) * 128],
                                    QT_sb[:, h * TPC + qch * 512:h * TPC + (qch + 1) * 512],
                                    start=True, stop=True)
                t_s = p.on("tensor", mms_, waits=[t_k, t_qt, pS_user[sb_]], inc="ps")
                if causal:
                    bias_ap = cmb[:, (qch * 64 + kc) * NH + h:(qch * 64 + kc) * NH + h + 1]
                    bias_wait = t_cmb
                else:
                    bias_ap = negMb[:, h:h + 1]
                    bias_wait = t_nbe

                def expf(s, sb_=sb_, pb=pb, qch=qch, bias_ap=bias_ap):
                    return s.activation(out=PT[pb][:, qch * 512:(qch + 1) * 512],
                                        in_=pS[sb_][:], func=AF.Exp,
                                        scale=float(1.0 / np.sqrt(128.0)),
                                        bias=bias_ap)
                t_e = p.on("scalar", expf, waits=[t_s, bias_wait, PT_user[pb]], inc="ss")
                pS_user[sb_] = t_e
                t_mask = None
                if causal and ((qch == 0 and kc < 4) or (qch == 1 and 4 <= kc < 8)):
                    j = kc if qch == 0 else kc - 4

                    def maskf(g, pb=pb, qch=qch, j=j):
                        return g.affine_select(
                            out=PT[pb][:, qch * 512:(qch + 1) * 512],
                            in_=PT[pb][:, qch * 512:(qch + 1) * 512],
                            compare_op=ALU.is_ge, fill=0.0, base=-j * 128,
                            pattern=[[1, 512]], channel_multiplier=-1)
                    t_mask = p.on("gpsimd", maskf, waits=[t_e], inc="gs")
                t_rdy = t_mask if t_mask is not None else t_e

                def mmpv(t, qch=qch, kc=kc, pb=pb, kb=kb, h=h):
                    ins = t.matmul(pA[:, qch * 512:(qch + 1) * 512],
                                   vtile[kb][:, kc * 128:(kc + 1) * 128],
                                   PT[pb][:, qch * 512:(qch + 1) * 512],
                                   start=(kc == 0), stop=(kc == 63),
                                   skip_group_check=True)
                    ins = t.matmul(psums[0:1, qch * 512:(qch + 1) * 512],
                                   onesb[:],
                                   PT[pb][:, qch * 512:(qch + 1) * 512],
                                   start=(kc == 0), stop=(kc == 63),
                                   skip_group_check=True)
                    return ins
                t_pv = p.on("tensor", mmpv,
                            waits=[t_rdy, t_v, t_onesb,
                                   pA_user if kc == 0 else None,
                                   psums_user if kc == 0 else None],
                            inc="ps")
                PT_user[pb] = t_pv
        t_rs = p.on("vector", lambda v: v.reciprocal(out=rsum1[:], in_=psums[0:1, :]),
                    waits=[PT_user[0], PT_user[1], ev_rs], inc="vs")
        rb_toks = []
        for half in range(2):
            t_rb = p.on("tensor", lambda t, half=half: t.matmul(
                pS[half][:], ones1[0:1, 0:128],
                rsum1[0:1, half * 512:(half + 1) * 512], start=True, stop=True),
                waits=[t_rs, pS_user[half]], inc="ps")
            t_re = p.on("vector", lambda v, half=half: v.tensor_copy(
                out=rsumb[:, half * 512:(half + 1) * 512], in_=pS[half][:]),
                waits=[t_rb, ev_attnT], inc="vs")
            pS_user[half] = t_re
            rb_toks.append(t_re)
        t_at = p.on("vector", lambda v, h=h: v.tensor_tensor(
            out=attnT[:, h * TPC:(h + 1) * TPC], in0=pA[:], in1=rsumb[:],
            op=ALU.mult), waits=rb_toks + [ev_attnT], inc="vs")
        pA_user = t_at
        psums_user = t_rs
        ev_attnT = t_at
        ev_rs = t_re
        kload_prev[kb] = t_pv
    attnT_ready = ev_attnT
    if dbg:
        td = p.on("gpsimd", lambda g: g.dma_start(out=dbg_at, in_=attnT[:]),
                  waits=[attnT_ready], inc="st", dma=True)
        p.on("gpsimd", lambda g: g.dma_start(out=dbg_nm, in_=negMb[:]),
             waits=[t_nbe], inc="st", dma=True)

    wo_cast_prev = [None, None]
    wo_mm_prev = None
    st_prev = None
    for jc4 in range(4):
        cast_toks = []
        for qu in range(8):
            bb = qu % 2
            t_w = p.on("sync", lambda s, jc4=jc4, qu=qu, bb=bb: s.dma_start(
                out=wost[bb][:],
                in_=Wo[:, jc4 * 512:(jc4 + 1) * 512]
                .rearrange("(c p) j -> p c j", p=128)[:, qu * 2:(qu + 1) * 2, :]),
                waits=[wo_cast_prev[bb], t_bob if jc4 == 0 and qu <= 1 else None],
                inc=f"dw{bb}", dma=True)
            t_c = p.on("vector", lambda v, qu=qu, bb=bb: v.tensor_copy(
                out=wocol[:, qu * 1024:(qu + 1) * 1024], in_=wost[bb][:]),
                waits=[t_w, wo_mm_prev if qu == 0 else None], inc="vs")
            cast_toks.append(t_c)
            wo_cast_prev[bb] = t_c
        w_rdy = cast_toks[-1]
        if dbg and jc4 == 0:
            w_rdy = p.on("gpsimd", lambda g: g.dma_start(out=dbg_wo, in_=wocol[:]),
                         waits=[w_rdy], inc="st", dma=True)
        for tt in range(NTT):
            bank = tt % 2

            def mmo(t, jc4=jc4, tt=tt, bank=bank):
                t.matmul(pO[bank][:], onesrow[:],
                         borbf[0:1, jc4 * 512:(jc4 + 1) * 512],
                         start=True, stop=False)
                for ic in range(NDC):
                    ins = t.matmul(pO[bank][:],
                                   attnT[:, ic * TPC + tt * 128:ic * TPC + (tt + 1) * 128],
                                   wocol[:, ic * 512:(ic + 1) * 512],
                                   start=False, stop=(ic == NDC - 1))
                return ins
            t_mo = p.on("tensor", mmo, waits=[w_rdy, attnT_ready, t_bob, t_onesr,
                                              st_prev,
                                              p.tok("vs") if ser_op else None,
                                              p.tok("ld") if ser_op else None], inc="ps")
            wo_mm_prev = t_mo
            t_x = p.on("sync", lambda s, tt=tt, jc4=jc4: s.dma_start(
                out=xt[:, jc4 * 512:(jc4 + 1) * 512],
                in_=xin[tt * 128:(tt + 1) * 128, jc4 * 512:(jc4 + 1) * 512]),
                waits=[st_prev, t_bob if (tt == 0) else None], inc="dx", dma=True)
            t_o2 = p.on("vector", lambda v, tt=tt, jc4=jc4, bank=bank: v.tensor_tensor(
                out=xo_sb[:, jc4 * 512:(jc4 + 1) * 512], in0=pO[bank][:],
                in1=xt[:, jc4 * 512:(jc4 + 1) * 512], op=ALU.add),
                waits=[t_mo, t_x, st_prev], inc="vs")
            st_prev = p.on("gpsimd", lambda g, tt=tt, jc4=jc4: g.dma_start(
                out=xout[tt * 128:(tt + 1) * 128, jc4 * 512:(jc4 + 1) * 512],
                in_=xo_sb[:, jc4 * 512:(jc4 + 1) * 512]),
                waits=[t_o2], inc="st", dma=True)
    p.barrier_wait("gpsimd", p.tok("st"))
    return p



def build_tern_a():
    p = Prog()
    nc = p.nc
    wg = nc.dram_tensor("wg_s", [256, 8192], F32, kind="ExternalInput").ap()
    wu = nc.dram_tensor("wu_s", [256, 8192], F32, kind="ExternalInput").ap()
    wd = nc.dram_tensor("wd_s", [1024, 2048], F32, kind="ExternalInput").ap()
    out = nc.dram_tensor("asum", [1, 3], F32, kind="ExternalOutput").ap()

    wt = [p.sb(f"wt{i}", [128, 8192], F32) for i in range(2)]
    rs = [p.sb(f"rs{i}", [128, 1], F32) for i in range(2)]
    onesc = p.sb("onesc", [128, 1], F32)
    osb = p.sb("osb", [1, 3], F32)
    pl = p.ps("pl", [1, 3], F32)

    t_on = p.on("gpsimd", lambda g: g.memset(onesc[:], 1.0), inc="gs")
    mats = [(wg, 2, 8192, 0), (wu, 2, 8192, 1), (wd, 8, 2048, 2)]
    rd_prev = [None, None]
    mm_prev = None
    for (src_, ntl, w, m) in mats:
        for i in range(ntl):
            bb = i % 2
            t_w = p.on("sync", lambda s, src_=src_, i=i, w=w, bb=bb: s.dma_start(
                out=wt[bb][:, 0:w], in_=src_[i * 128:(i + 1) * 128, :]),
                waits=[rd_prev[bb]], inc=f"dw{bb}", dma=True)
            t_r = p.on("vector", lambda v, bb=bb, w=w: v.reduce_sum(
                out=rs[bb][:], in_=wt[bb][:, 0:w], axis=AX.X,
                apply_absolute_value=True), waits=[t_w, mm_prev], inc="vs")
            rd_prev[bb] = t_r
            mm_prev = p.on("tensor", lambda t, bb=bb, m=m, i=i, ntl=ntl: t.matmul(
                pl[0:1, m:m + 1], rs[bb][:], onesc[:],
                start=(i == 0), stop=(i == ntl - 1)),
                waits=[t_r, t_on], inc="ps")
    t_e = p.on("vector", lambda v: v.tensor_copy(out=osb[:], in_=pl[0:1, :]),
               waits=[mm_prev], inc="vs")
    t_s = p.on("gpsimd", lambda g: g.dma_start(out=out, in_=osb[:]),
               waits=[t_e], inc="st", dma=True)
    p.barrier_wait("gpsimd", t_s)
    return p



def build_tern_b():
    p = Prog()
    nc = p.nc
    wg = nc.dram_tensor("wg_s", [256, 8192], F32, kind="ExternalInput").ap()
    wu = nc.dram_tensor("wu_s", [256, 8192], F32, kind="ExternalInput").ap()
    wd = nc.dram_tensor("wd_s", [1024, 2048], F32, kind="ExternalInput").ap()
    parts = nc.dram_tensor("parts", [NCORE, 3], F32, kind="ExternalInput").ap()
    tg = nc.dram_tensor("tg_s", [256, 8192], BF16, kind="ExternalOutput").ap()
    tu = nc.dram_tensor("tu_s", [256, 8192], BF16, kind="ExternalOutput").ap()
    td = nc.dram_tensor("td_s", [1024, 2048], BF16, kind="ExternalOutput").ap()
    s3o = nc.dram_tensor("s3", [1, 3], F32, kind="ExternalOutput").ap()

    wt = [p.sb(f"wt{i}", [128, 8192], F32) for i in range(2)]
    tf = p.sb("tf", [128, 8192], F32)
    ti = p.sb("ti", [128, 8192], I32)
    tb = [p.sb(f"tb{i}", [128, 8192], BF16) for i in range(2)]
    psb = p.sb("psb", [NCORE, 3], F32)
    ones8 = p.sb("ones8", [NCORE, 1], F32)
    ones1r = p.sb("ones1r", [1, 128], F32)
    identf = p.sb("identf", [8, 8], F32)
    s3col = p.sb("s3col", [3, 1], F32)
    s3row = p.sb("s3row", [1, 3], F32)
    s3m = p.sb("s3m", [1, 3], F32)
    s3i = p.sb("s3i", [1, 3], F32)
    invb = p.sb("invb", [128, 3], F32)
    pm = p.ps("pm", [128, 3], F32)

    t_on = p.on("gpsimd", lambda g: g.memset(ones8[:], 1.0), inc="gs")
    t_o1 = p.on("gpsimd", lambda g: g.memset(ones1r[:], 1.0), inc="gs")
    p.on("gpsimd", lambda g: g.memset(identf[:], 0.0))
    t_idf = p.on("gpsimd", lambda g: g.affine_select(
        out=identf[:], in_=identf[:], compare_op=ALU.not_equal, fill=1.0,
        base=0, pattern=[[-1, 8]], channel_multiplier=1), inc="gs")
    t_p = p.on("sync", lambda s: s.dma_start(out=psb[:], in_=parts), inc="dm", dma=True)
    t_m1 = p.on("tensor", lambda t: t.matmul(pm[0:3, 0:1], psb[:], ones8[:],
                                             start=True, stop=True),
                waits=[t_p, t_on], inc="ps")
    t_e1 = p.on("vector", lambda v: v.tensor_copy(out=s3col[:], in_=pm[0:3, 0:1]),
                waits=[t_m1], inc="vs")
    t_m2 = p.on("tensor", lambda t: t.transpose(pm[0:1, 0:3], s3col[:], identf[0:3, 0:3]),
                waits=[t_e1, t_idf], inc="ps")
    t_e2 = p.on("vector", lambda v: v.tensor_copy(out=s3row[:], in_=pm[0:1, 0:3]),
                waits=[t_m2], inc="vs")
    t_mn = p.on("vector", lambda v: v.tensor_scalar(
        out=s3m[:], in0=s3row[:], scalar1=1.0 / 16777216.0, scalar2=EPS,
        op0=ALU.mult, op1=ALU.max), waits=[t_e2], inc="vs")
    t_iv = p.on("vector", lambda v: v.reciprocal(out=s3i[:], in_=s3m[:]),
                waits=[t_mn], inc="vs")
    bw = []
    prevmm = [t_m2]
    for m in range(3):
        t_b = p.on("tensor", lambda t, m=m: t.matmul(
            pm[:, m:m + 1], ones1r[:], s3i[0:1, m:m + 1], start=True, stop=True),
            waits=[t_iv, t_o1] + prevmm, inc="ps")
        prevmm = [t_b]
        bw.append(t_b)
    t_ib = p.on("vector", lambda v: v.tensor_copy(out=invb[:], in_=pm[:, 0:3]),
                waits=[bw[-1]], inc="vs")

    mats = [(wg, tg, 2, 8192, 0), (wu, tu, 2, 8192, 1), (wd, td, 8, 2048, 2)]
    rd_prev = [None, None]
    st_prev = [None, None]
    ch_prev = None
    for (src_, dst_, ntl, w, m) in mats:
        for i in range(ntl):
            bb = i % 2
            t_w = p.on("sync", lambda s, src_=src_, i=i, w=w, bb=bb: s.dma_start(
                out=wt[bb][:, 0:w], in_=src_[i * 128:(i + 1) * 128, :]),
                waits=[rd_prev[bb]], inc=f"dw{bb}", dma=True)
            t_1 = p.on("vector", lambda v, bb=bb, w=w, m=m: v.tensor_scalar(
                out=tf[:, 0:w], in0=wt[bb][:, 0:w], scalar1=invb[:, m:m + 1],
                scalar2=1.0, op0=ALU.mult, op1=ALU.min),
                waits=[t_w, t_ib, ch_prev], inc="vs")
            rd_prev[bb] = t_1
            t_2 = p.on("vector", lambda v, w=w: v.tensor_scalar(
                out=ti[:, 0:w], in_=None, in0=tf[:, 0:w], scalar1=-1.0,
                scalar2=None, op0=ALU.max) if False else v.tensor_scalar(
                out=ti[:, 0:w], in0=tf[:, 0:w], scalar1=-1.0, scalar2=None,
                op0=ALU.max), waits=[t_1], inc="vs")
            t_3 = p.on("vector", lambda v, bb=bb, w=w: v.tensor_copy(
                out=tb[bb][:, 0:w], in_=ti[:, 0:w]),
                waits=[t_2, st_prev[bb]], inc="vs")
            ch_prev = t_3
            st_prev[bb] = p.on("gpsimd", lambda g, dst_=dst_, i=i, w=w, bb=bb: g.dma_start(
                out=dst_[i * 128:(i + 1) * 128, :], in_=tb[bb][:, 0:w]),
                waits=[t_3], inc=f"st{bb}", dma=True)
    t_s3 = p.on("gpsimd", lambda g: g.dma_start(out=s3o, in_=s3m[:]),
                waits=[t_mn], inc="st0", dma=True)
    p.barrier_wait("gpsimd", p.tok("st0"), p.tok("st1"))
    return p



OCH = 256
C2048 = float(1.0 / np.sqrt(2048.0))
C8192 = float(1.0 / np.sqrt(8192.0))


def build_ffn1a():
    p = Prog()
    nc = p.nc
    xin = nc.dram_tensor("x", [TPC, D], F32, kind="ExternalInput").ap()
    H2 = nc.dram_tensor("H2048", [D, D], BF16, kind="ExternalInput").ap()
    tgT = nc.dram_tensor("tgT", [D, 8192], BF16, kind="ExternalInput").ap()
    tuT = nc.dram_tensor("tuT", [D, 8192], BF16, kind="ExternalInput").ap()
    s3 = nc.dram_tensor("s3", [1, 3], F32, kind="ExternalInput").ap()
    hout = nc.dram_tensor("h", [TPC, 8192], BF16, kind="ExternalOutput").ap()

    xt = p.sb("xt", [128, D], F32)
    ybf = p.sb("ybf", [128, D], BF16)
    yT = p.sb("yT", [128, NDC * TPC], BF16)
    identb = p.sb("identb", [128, 128], BF16)
    Hsb = p.sb("Hsb", [128, NDC * D], BF16)
    wsb = p.sb("wsb", [128, D], F32)
    qi = p.sb("qi", [128, D], I32)
    qtm = p.sb("qtm", [128, D], BF16)
    amx = p.sb("amx", [128, 1], F32)
    mc = p.sb("mc", [128, 1], F32)
    rmc = p.sb("rmc", [128, 1], F32)
    sct = p.sb("sct", [128, 1], F32)
    invq = p.sb("invq", [128, NTT], F32)
    s3sb = p.sb("s3sb", [1, 3], F32)
    sgb = p.sb("sgb", [128, 2], F32)
    kg = p.sb("kg", [128, 1], F32)
    ku = p.sb("ku", [128, 1], F32)
    ones1r = p.sb("ones1r", [1, 128], F32)
    gsl = [p.sb(f"gsl{i}", [128, NDC * OCH], BF16) for i in range(2)]
    usl = [p.sb(f"usl{i}", [128, NDC * OCH], BF16) for i in range(2)]
    hsil = p.sb("hsil", [128, OCH], F32)
    hst = [p.sb(f"hst{i}", [128, OCH], BF16) for i in range(2)]
    ln = LNCtx(p)

    pyT = p.ps("pyT", [128, D], BF16)
    pw = [p.ps(f"pw{i}", [128, 512], F32) for i in range(2)]
    pg = p.ps("pg", [128, OCH], F32)
    pu = p.ps("pu", [128, OCH], F32)
    pb = p.ps("pb", [128, 2], F32)

    t_id = emit_identity(p, identb)
    t_o1 = p.on("gpsimd", lambda g: g.memset(ones1r[:], 1.0), inc="gs")
    p.on("sync", lambda s: s.dma_start(out=s3sb[:], in_=s3), inc="dm", dma=True)
    t_H = p.on("sync", lambda s: s.dma_start(
        out=Hsb[:], in_=H2.rearrange("(c p) j -> p c j", p=128)), inc="dm", dma=True)
    t_misc = p.tok("dm")
    t_bc = p.on("tensor", lambda t: t.matmul(pb[:], ones1r[:], s3sb[0:1, 0:2],
                                             start=True, stop=True),
                waits=[t_misc, t_o1], inc="ps")
    t_sg = p.on("vector", lambda v: v.tensor_copy(out=sgb[:], in_=pb[:]),
                waits=[t_bc], inc="vs")

    yT_ready = emit_ln_transpose_phase(p, ln, xin, xt, ybf, yT, pyT, identb, t_id)

    pw_user = [None, None]
    q_done = [None] * NTT
    qtr_prev = None
    for tt in range(NTT):
        ev_toks = []
        for ch in range(4):
            bank = ch % 2

            def mmw(t, tt=tt, ch=ch, bank=bank):
                for fc in range(NDC):
                    ins = t.matmul(pw[bank][:],
                                   yT[:, fc * TPC + tt * 128:fc * TPC + (tt + 1) * 128],
                                   Hsb[:, fc * D + ch * 512:fc * D + (ch + 1) * 512],
                                   start=(fc == 0), stop=(fc == NDC - 1))
                return ins
            t_mw = p.on("tensor", mmw, waits=[yT_ready, t_misc, pw_user[bank]],
                        inc="ps")
            t_ev = p.on("vector", lambda v, ch=ch, bank=bank: v.tensor_copy(
                out=wsb[:, ch * 512:(ch + 1) * 512], in_=pw[bank][:]),
                waits=[t_mw], inc="vs")
            pw_user[bank] = t_ev
            ev_toks.append(t_ev)
        t_am = p.on("vector", lambda v: v.reduce_max(
            out=amx[:], in_=wsb[:], axis=AX.X, apply_absolute_value=True),
            waits=ev_toks, inc="vs")
        t_mc = p.on("vector", lambda v: v.tensor_scalar(
            out=mc[:], in0=amx[:], scalar1=C2048, scalar2=EPS,
            op0=ALU.mult, op1=ALU.max), waits=[t_am], inc="vs")
        t_rm = p.on("vector", lambda v: v.reciprocal(out=rmc[:], in_=mc[:]),
                    waits=[t_mc], inc="vs")
        t_sc = p.on("vector", lambda v: v.tensor_scalar_mul(
            sct[:], rmc[:], 127.0 * C2048), waits=[t_rm], inc="vs")
        t_in = p.on("vector", lambda v, tt=tt: v.tensor_scalar_mul(
            invq[:, tt:tt + 1], mc[:], 1.0 / 127.0), waits=[t_mc], inc="vs")
        t_q = p.on("vector", lambda v: v.tensor_scalar(
            out=qi[:], in0=wsb[:], scalar1=sct[:, 0:1], scalar2=None,
            op0=ALU.mult), waits=[t_sc], inc="vs")
        t_qb = p.on("vector", lambda v: v.tensor_copy(out=qtm[:], in_=qi[:]),
                    waits=[t_q], inc="vs")

        def qtr(t, tt=tt):
            for fc in range(NDC):
                ins = t.transpose(pyT[:, fc * 128:(fc + 1) * 128],
                                  qtm[:, fc * 128:(fc + 1) * 128], identb[:])
            return ins
        t_tr = p.on("tensor", qtr, waits=[t_qb, t_mw, qtr_prev], inc="ps")

        def qcp(v, tt=tt):
            return v.tensor_copy(
                out=yT[:].rearrange("p (c n) -> p c n", c=NDC)[:, :, tt * 128:(tt + 1) * 128],
                in_=pyT[:].rearrange("p (c n) -> p c n", c=NDC))
        q_done[tt] = p.on("vector", qcp, waits=[t_tr], inc="vs")
        qtr_prev = q_done[tt]
    qT_ready = q_done[-1]

    g_cast_prev = [None, None]
    kg_done = None
    st_prev = [None, None]
    pg_user = None
    pu_user = None
    for oc in range(8192 // OCH):
        bb = oc % 2
        t_wg = p.on("sync", lambda s, oc=oc, bb=bb: s.dma_start(
            out=gsl[bb][:], in_=tgT[:, oc * OCH:(oc + 1) * OCH]
            .rearrange("(c p) j -> p c j", p=128)),
            waits=[g_cast_prev[bb]], inc=f"dg{bb}", dma=True)
        t_wu = p.on("sync", lambda s, oc=oc, bb=bb: s.dma_start(
            out=usl[bb][:], in_=tuT[:, oc * OCH:(oc + 1) * OCH]
            .rearrange("(c p) j -> p c j", p=128)),
            waits=[g_cast_prev[bb]], inc=f"du{bb}", dma=True)
        last_mm = None
        for tt in range(NTT):
            def mmg(t, oc=oc, tt=tt, bb=bb):
                for fc in range(NDC):
                    t.matmul(pg[:],
                             yT[:, fc * TPC + tt * 128:fc * TPC + (tt + 1) * 128],
                             gsl[bb][:, fc * OCH:(fc + 1) * OCH],
                             start=(fc == 0), stop=(fc == NDC - 1))
                for fc in range(NDC):
                    ins = t.matmul(pu[:],
                                   yT[:, fc * TPC + tt * 128:fc * TPC + (tt + 1) * 128],
                                   usl[bb][:, fc * OCH:(fc + 1) * OCH],
                                   start=(fc == 0), stop=(fc == NDC - 1))
                return ins
            t_mm = p.on("tensor", mmg, waits=[t_wg, t_wu, qT_ready,
                                              pg_user, pu_user], inc="ps")
            last_mm = t_mm
            if oc == 0:
                t_kg = p.on("vector", lambda v, tt=tt: v.tensor_tensor(
                    out=kg[:], in0=sgb[:, 0:1], in1=invq[:, tt:tt + 1],
                    op=ALU.mult), waits=[t_sg, q_done[tt], kg_done], inc="vs")
                t_ku = p.on("vector", lambda v, tt=tt: v.tensor_tensor(
                    out=ku[:], in0=sgb[:, 1:2], in1=invq[:, tt:tt + 1],
                    op=ALU.mult), waits=[t_kg], inc="vs")
            else:
                t_kg = p.on("vector", lambda v, tt=tt: v.tensor_tensor(
                    out=kg[:], in0=sgb[:, 0:1], in1=invq[:, tt:tt + 1],
                    op=ALU.mult), waits=[t_sg, kg_done], inc="vs")
                t_ku = p.on("vector", lambda v, tt=tt: v.tensor_tensor(
                    out=ku[:], in0=sgb[:, 1:2], in1=invq[:, tt:tt + 1],
                    op=ALU.mult), waits=[t_kg], inc="vs")
            t_si = p.on("scalar", lambda s, tt=tt: s.activation(
                out=hsil[:], in_=pg[:], func=AF.Silu, scale=kg[:, 0:1]),
                waits=[t_mm, t_ku], inc="ss")
            pg_user = t_si
            t_h = p.on("vector", lambda v, bb=bb: v.scalar_tensor_tensor(
                out=hst[bb][:], in0=pu[:], scalar=ku[:, 0:1], in1=hsil[:],
                op0=ALU.mult, op1=ALU.mult),
                waits=[t_si, st_prev[bb]], inc="vs")
            pu_user = t_h
            kg_done = t_h
            st_prev[bb] = p.on("gpsimd", lambda g, oc=oc, tt=tt, bb=bb: g.dma_start(
                out=hout[tt * 128:(tt + 1) * 128, oc * OCH:(oc + 1) * OCH],
                in_=hst[bb][:]), waits=[t_h], inc=f"sh{bb}", dma=True)
        g_cast_prev[bb] = last_mm
    p.barrier_wait("gpsimd", p.tok("sh0"), p.tok("sh1"))
    return p



def build_ffn1b():
    p = Prog()
    nc = p.nc
    xin = nc.dram_tensor("x", [TPC, D], F32, kind="ExternalInput").ap()
    hin = nc.dram_tensor("h", [TPC, 8192], BF16, kind="ExternalInput").ap()
    H1 = nc.dram_tensor("H128", [128, 128], BF16, kind="ExternalInput").ap()
    tdT = nc.dram_tensor("tdT", [8192, D], BF16, kind="ExternalInput").ap()
    s3 = nc.dram_tensor("s3", [1, 3], F32, kind="ExternalInput").ap()
    xout = nc.dram_tensor("xo", [TPC, D], F32, kind="ExternalOutput").ap()

    hrow = [p.sb(f"hrow{i}", [128, 8192], BF16) for i in range(2)]
    hT = p.sb("hT", [128, 8192], BF16)
    zb = p.sb("zb", [128, 8192], F32)
    zb2 = p.sb("zb2", [128, 8192], F32)
    qd = p.sb("qd", [128, 8192], BF16)
    qi = p.sb("qi", [128, 2048], I32)
    qdT = p.sb("qdT", [128, 8192], BF16)
    tds = [p.sb(f"tds{i}", [128, D], BF16) for i in range(2)]
    H1sb = p.sb("H1sb", [128, 128], BF16)
    identb = p.sb("identb", [128, 128], BF16)
    amx = p.sb("amx", [128, 1], F32)
    mc = p.sb("mc", [128, 1], F32)
    rmc = p.sb("rmc", [128, 1], F32)
    sct = p.sb("sct", [128, 1], F32)
    kd = p.sb("kd", [128, 1], F32)
    s3sb = p.sb("s3sb", [1, 3], F32)
    sdb = p.sb("sdb", [128, 1], F32)
    ones1r = p.sb("ones1r", [1, 128], F32)
    xt = p.sb("xt", [128, D], F32)
    xo_sb = p.sb("xo_sb", [128, D], F32)

    pA_ = p.ps("pA_", [128, 1024], BF16)
    pz = p.ps("pz", [128, 512], F32)
    peo = p.ps("peo", [128, D], F32)

    t_id = emit_identity(p, identb)
    t_o1 = p.on("gpsimd", lambda g: g.memset(ones1r[:], 1.0), inc="gs")
    p.on("sync", lambda s: s.dma_start(out=s3sb[:], in_=s3), inc="dm", dma=True)
    p.on("sync", lambda s: s.dma_start(out=H1sb[:], in_=H1), inc="dm", dma=True)
    t_misc = p.tok("dm")
    t_bc = p.on("tensor", lambda t: t.matmul(pz[:, 0:1], ones1r[:], s3sb[0:1, 2:3],
                                             start=True, stop=True),
                waits=[t_misc, t_o1], inc="ps")
    t_sd = p.on("vector", lambda v: v.tensor_copy(out=sdb[:], in_=pz[:, 0:1]),
                waits=[t_bc], inc="vs")

    h_cons = [None, None]
    tr_user = None
    pz_user = t_sd
    st_prev = None
    td_cast_prev = [None, None]
    down_done = None
    for tt in range(NTT):
        bb = tt % 2
        t_h = p.on("sync", lambda s, tt=tt, bb=bb: s.dma_start(
            out=hrow[bb][:], in_=hin[tt * 128:(tt + 1) * 128, :]),
            waits=[h_cons[bb]], inc=f"dh{bb}", dma=True)
        tr_toks = []
        for grp in range(8):
            def htr(t, grp=grp, bb=bb):
                for u in range(8):
                    a = grp * 8 + u
                    ins = t.transpose(pA_[:, u * 128:(u + 1) * 128],
                                      hrow[bb][:, a * 128:(a + 1) * 128], identb[:])
                return ins
            t_tr = p.on("tensor", htr, waits=[t_h, t_id, tr_user], inc="ps")
            t_cp = p.on("vector", lambda v, grp=grp: v.tensor_copy(
                out=hT[:, grp * 1024:(grp + 1) * 1024], in_=pA_[:]),
                waits=[t_tr], inc="vs")
            tr_user = t_cp
            tr_toks.append(t_cp)
        h_cons[bb] = tr_toks[-1]
        zev = []
        for a4 in range(16):
            def mza(t, a4=a4):
                for u in range(4):
                    a = a4 * 4 + u
                    ins = t.matmul(pz[:, u * 128:(u + 1) * 128],
                                   hT[:, a * 128:(a + 1) * 128], H1sb[:],
                                   start=True, stop=True)
                return ins
            t_za = p.on("tensor", mza,
                        waits=[tr_toks[a4 // 2], t_misc, pz_user], inc="ps")
            t_ze = p.on("vector", lambda v, a4=a4: v.tensor_copy(
                out=zb[:, a4 * 512:(a4 + 1) * 512], in_=pz[:]),
                waits=[t_za], inc="vs")
            pz_user = t_ze
            zev.append(t_ze)
        bt = zev[-1]
        bufs = [zb, zb2]
        cur = 0
        for pi, hh in enumerate([1, 2, 4, 8, 16, 32]):
            src_b, dst_b = bufs[cur], bufs[1 - cur]
            nblk = 64 // (2 * hh)

            def bfly(v, src_b=src_b, dst_b=dst_b, hh=hh, nblk=nblk):
                sv = src_b[:].rearrange("p (blk two h d) -> p blk two h d",
                                        blk=nblk, two=2, h=hh)
                dv = dst_b[:].rearrange("p (blk two h d) -> p blk two h d",
                                        blk=nblk, two=2, h=hh)
                v.tensor_tensor(out=dv[:, :, 0, :, :], in0=sv[:, :, 0, :, :],
                                in1=sv[:, :, 1, :, :], op=ALU.add)
                return v.tensor_tensor(out=dv[:, :, 1, :, :], in0=sv[:, :, 0, :, :],
                                       in1=sv[:, :, 1, :, :], op=ALU.subtract)
            bt = p.on("vector", bfly, waits=[bt] + (zev if pi == 0 else []), inc="vs")
            cur = 1 - cur
        zfin = bufs[cur]
        t_am = p.on("vector", lambda v, zfin=zfin: v.reduce_max(
            out=amx[:], in_=zfin[:], axis=AX.X, apply_absolute_value=True),
            waits=[bt], inc="vs")
        t_mc = p.on("vector", lambda v: v.tensor_scalar(
            out=mc[:], in0=amx[:], scalar1=C8192, scalar2=EPS,
            op0=ALU.mult, op1=ALU.max), waits=[t_am], inc="vs")
        t_rm = p.on("vector", lambda v: v.reciprocal(out=rmc[:], in_=mc[:]),
                    waits=[t_mc], inc="vs")
        t_sc = p.on("vector", lambda v: v.tensor_scalar_mul(
            sct[:], rmc[:], 127.0 * C8192), waits=[t_rm], inc="vs")
        t_kd = p.on("vector", lambda v: v.scalar_tensor_tensor(
            out=kd[:], in0=mc[:], scalar=0.5 / 127.0, in1=sdb[:],
            op0=ALU.mult, op1=ALU.mult), waits=[t_mc, t_sd, down_done], inc="vs")
        qts = []
        for ch in range(4):
            t_q = p.on("vector", lambda v, ch=ch, zfin=zfin: v.tensor_scalar(
                out=qi[:], in0=zfin[:, ch * 2048:(ch + 1) * 2048],
                scalar1=sct[:, 0:1], scalar2=None, op0=ALU.mult),
                waits=[t_sc, qts[-1] if qts else None], inc="vs")
            t_qb = p.on("vector", lambda v, ch=ch: v.tensor_copy(
                out=qd[:, ch * 2048:(ch + 1) * 2048], in_=qi[:]),
                waits=[t_q], inc="vs")
            qts.append(t_qb)
        qtr_toks = []
        for grp in range(8):
            def qtr(t, grp=grp):
                for u in range(8):
                    a = grp * 8 + u
                    ins = t.transpose(pA_[:, u * 128:(u + 1) * 128],
                                      qd[:, a * 128:(a + 1) * 128], identb[:])
                return ins
            t_tr = p.on("tensor", qtr, waits=[qts[(grp * 8) // 16], tr_user, qts[-1]],
                        inc="ps")
            t_cp = p.on("vector", lambda v, grp=grp: v.tensor_copy(
                out=qdT[:, grp * 1024:(grp + 1) * 1024], in_=pA_[:]),
                waits=[t_tr], inc="vs")
            tr_user = t_cp
            qtr_toks.append(t_cp)
        qdT_ready = qtr_toks[-1]
        last_dmm = None
        for i in range(64):
            ib = i % 2
            t_td = p.on("sync", lambda s, i=i, ib=ib: s.dma_start(
                out=tds[ib][:], in_=tdT[i * 128:(i + 1) * 128, :]),
                waits=[td_cast_prev[ib]], inc=f"dt{ib}", dma=True)

            def mmd(t, i=i, ib=ib):
                for dch in range(4):
                    ins = t.matmul(peo[:, dch * 512:(dch + 1) * 512],
                                   qdT[:, i * 128:(i + 1) * 128],
                                   tds[ib][:, dch * 512:(dch + 1) * 512],
                                   start=(i == 0), stop=(i == 63))
                return ins
            last_dmm = p.on("tensor", mmd, waits=[t_td, qdT_ready,
                                                  down_done if i <= 1 else None],
                            inc="ps")
            td_cast_prev[ib] = last_dmm
        t_x = p.on("sync", lambda s, tt=tt: s.dma_start(
            out=xt[:], in_=xin[tt * 128:(tt + 1) * 128, :]),
            waits=[st_prev], inc="dx", dma=True)
        ev = []
        for dch in range(4):
            t_e = p.on("vector", lambda v, dch=dch: v.scalar_tensor_tensor(
                out=xo_sb[:, dch * 512:(dch + 1) * 512],
                in0=peo[:, dch * 512:(dch + 1) * 512], scalar=kd[:, 0:1],
                in1=xt[:, dch * 512:(dch + 1) * 512], op0=ALU.mult, op1=ALU.add),
                waits=[last_dmm, t_kd, t_x, st_prev], inc="vs")
            ev.append(t_e)
        down_done = ev[-1]
        st_prev = p.on("gpsimd", lambda g, tt=tt: g.dma_start(
            out=xout[tt * 128:(tt + 1) * 128, :], in_=xo_sb[:]),
            waits=ev, inc="st", dma=True)
    p.barrier_wait("gpsimd", p.tok("st"))
    return p



def _run(p, in_maps):
    nc = p.build()
    res = run_bass_kernel_spmd(nc, in_maps, core_ids=list(range(NCORE)))
    p.close() if hasattr(p, "close") else p.es.close()
    return res.results


def _moe_sublayer(xs, gate_W, W1, b1, W2, b2):
    gateWT = np.ascontiguousarray(gate_W.T)
    b1T = np.ascontiguousarray(b1.T)
    res = _run(build_gate_sums(), [{"x": xs[c]} for c in range(NCORE)])
    parts = [res[c]["psums"][0] for c in range(NCORE)]
    partsT = np.ascontiguousarray(np.stack(parts, 1))
    res = _run(build_gate_coefs(),
               [{"partsT": partsT, "gateWT": gateWT} for _ in range(NCORE)])
    c6 = res[0]["c6"]
    c6p = np.ascontiguousarray(c6.reshape(6, 1))
    res = _run(build_moe_main(),
               [{"x": xs[c], "c6p": c6p, "c6r": c6, "W1": W1, "b1T": b1T,
                 "W2": W2, "b2p": b2} for c in range(NCORE)])
    return [res[c]["xo"] for c in range(NCORE)]


def _hadamard(n):
    H = np.array([[1.0]], np.float32)
    while H.shape[0] < n:
        H = np.block([[H, H], [H, -H]]).astype(np.float32)
    return H


def _tern_prep(w_gate, w_up, w_down):
    wgT = np.ascontiguousarray(w_gate.T)
    wuT = np.ascontiguousarray(w_up.T)
    wdT = np.ascontiguousarray(w_down.T)
    im = [{"wg_s": wgT[256 * c:256 * (c + 1)], "wu_s": wuT[256 * c:256 * (c + 1)],
           "wd_s": wdT[1024 * c:1024 * (c + 1)]} for c in range(NCORE)]
    res = _run(build_tern_a(), im)
    parts = np.ascontiguousarray(np.concatenate([res[c]["asum"] for c in range(NCORE)], 0))
    for c in range(NCORE):
        im[c]["parts"] = parts
    res = _run(build_tern_b(), im)
    tgT = np.ascontiguousarray(np.concatenate([res[c]["tg_s"] for c in range(NCORE)], 0))
    tuT = np.ascontiguousarray(np.concatenate([res[c]["tu_s"] for c in range(NCORE)], 0))
    tdT = np.ascontiguousarray(np.concatenate([res[c]["td_s"] for c in range(NCORE)], 0))
    s3 = res[0]["s3"]
    return tgT, tuT, tdT, s3


_H2048B = None
_H128B = None


def _l1ffn_sublayer(xs, tern):
    global _H2048B, _H128B
    if _H2048B is None:
        _H2048B = np.ascontiguousarray(_hadamard(2048).astype(BFNP))
        _H128B = np.ascontiguousarray(_hadamard(128).astype(BFNP))
    tgT, tuT, tdT, s3 = tern
    res = _run(build_ffn1a(),
               [{"x": xs[c], "H2048": _H2048B, "tgT": tgT, "tuT": tuT, "s3": s3}
                for c in range(NCORE)])
    hs = [res[c]["h"] for c in range(NCORE)]
    res = _run(build_ffn1b(),
               [{"x": xs[c], "h": hs[c], "H128": _H128B, "tdT": tdT, "s3": s3}
                for c in range(NCORE)])
    return [res[c]["xo"] for c in range(NCORE)]


_ORDERS = [[c, 15 - c] + [g for g in range(16) if g not in (c, 15 - c)]
           for c in range(NCORE)]


def _chunk_src(g):
    return (g, slice(0, 512)) if g < 8 else (15 - g, slice(512, 1024))


_PROGS = {}


def _prog(key, builder, *args):
    if key not in _PROGS:
        p = builder(*args)
        p.build()
        _PROGS[key] = p
    return _PROGS[key]


def _run_cached(key, builder, args, in_maps):
    p = _prog(key, builder, *args)
    res = run_bass_kernel_spmd(p.nc, in_maps, core_ids=list(range(NCORE)))
    return res.results


def _attn_sublayer(xs, Wf, bf, Wo, bo, dq, dkv, n_kv, causal, tag):
    grp = 16 // n_kv
    bTl = np.ascontiguousarray(bf.reshape(-1, 128).T)
    res = _run_cached(f"qkv{tag}", build_qkv, (dq, dkv),
                      [{"x": xs[c], "Wf": Wf, "bTl": bTl} for c in range(NCORE)])
    QT = [res[c]["QT"] for c in range(NCORE)]
    KT = [res[c]["KT"] for c in range(NCORE)]
    V = [res[c]["V"] for c in range(NCORE)]
    qn2 = [res[c]["qn2"] for c in range(NCORE)]
    kn2 = [res[c]["kn2"] for c in range(NCORE)]
    kn2f = np.ascontiguousarray(np.concatenate(
        [np.repeat(kn2[c][0], grp)[None, :] for c in range(NCORE)], 1))
    bor = np.ascontiguousarray(bo.reshape(1, -1))
    in_maps = []
    for c in range(NCORE):
        KTg = np.concatenate([KT[r][:, sl] for (r, sl) in map(_chunk_src, _ORDERS[c])], 1)
        Vg = np.concatenate([V[r][sl, :] for (r, sl) in map(_chunk_src, _ORDERS[c])], 0)
        m = {"x": xs[c], "QT": QT[c], "KTg": np.ascontiguousarray(KTg),
             "Vg": np.ascontiguousarray(Vg), "qn2": qn2[c], "kn2f": kn2f,
             "Wo": Wo, "bor": bor}
        if causal:
            mb = np.zeros((128, 128), np.float32)
            for qch in range(2):
                Gq = c if qch == 0 else 15 - c
                for kc in range(64):
                    G = _ORDERS[c][kc // 4]
                    if G != Gq and G > Gq:
                        mb[:, qch * 64 + kc] = -60000.0
            m["maskb"] = mb
        in_maps.append(m)
    res = _run_cached(f"attn{tag}", build_attn, (causal, n_kv), in_maps)
    return [res[c]["xo"] for c in range(NCORE)]


def _moe_sublayer2(xs, gateWT, W1, b1T, W2, b2):
    res = _run_cached("gsum", build_gate_sums, (),
                      [{"x": xs[c]} for c in range(NCORE)])
    parts = [res[c]["psums"][0] for c in range(NCORE)]
    partsT = np.ascontiguousarray(np.stack(parts, 1))
    res = _run_cached("gcoef", build_gate_coefs, (),
                      [{"partsT": partsT, "gateWT": gateWT} for _ in range(NCORE)])
    c6 = res[0]["c6"]
    c6p = np.ascontiguousarray(c6.reshape(6, 1))
    res = _run_cached("moe", build_moe_main, (),
                      [{"x": xs[c], "c6p": c6p, "c6r": c6, "W1": W1, "b1T": b1T,
                        "W2": W2, "b2p": b2} for c in range(NCORE)])
    return [res[c]["xo"] for c in range(NCORE)]


def _l1ffn_sublayer2(xs, tern):
    global _H2048B, _H128B
    if _H2048B is None:
        _H2048B = np.ascontiguousarray(_hadamard(2048).astype(BFNP))
        _H128B = np.ascontiguousarray(_hadamard(128).astype(BFNP))
    tgT, tuT, tdT, s3 = tern
    res = _run_cached("f1a", build_ffn1a, (),
                      [{"x": xs[c], "H2048": _H2048B, "tgT": tgT, "tuT": tuT,
                        "s3": s3} for c in range(NCORE)])
    hs = [res[c]["h"] for c in range(NCORE)]
    res = _run_cached("f1b", build_ffn1b, (),
                      [{"x": xs[c], "h": hs[c], "H128": _H128B, "tdT": tdT,
                        "s3": s3} for c in range(NCORE)])
    return [res[c]["xo"] for c in range(NCORE)]


def _tern_prep2(w_gate, w_up, w_down):
    wgT = np.ascontiguousarray(w_gate.T)
    wuT = np.ascontiguousarray(w_up.T)
    wdT = np.ascontiguousarray(w_down.T)
    im = [{"wg_s": wgT[256 * c:256 * (c + 1)], "wu_s": wuT[256 * c:256 * (c + 1)],
           "wd_s": wdT[1024 * c:1024 * (c + 1)]} for c in range(NCORE)]
    res = _run_cached("terna", build_tern_a, (), im)
    parts = np.ascontiguousarray(np.concatenate([res[c]["asum"] for c in range(NCORE)], 0))
    for c in range(NCORE):
        im[c]["parts"] = parts
    res = _run_cached("ternb", build_tern_b, (), im)
    tgT = np.ascontiguousarray(np.concatenate([res[c]["tg_s"] for c in range(NCORE)], 0))
    tuT = np.ascontiguousarray(np.concatenate([res[c]["tu_s"] for c in range(NCORE)], 0))
    tdT = np.ascontiguousarray(np.concatenate([res[c]["td_s"] for c in range(NCORE)], 0))
    return tgT, tuT, tdT, res[0]["s3"]


def kernel(L0_attn_x, L0_attn_Wqkv, L0_attn_bqkv, L0_attn_Wo, L0_attn_bo,
           L0_ffn_gate_W, L0_ffn_W1, L0_ffn_b1, L0_ffn_W2, L0_ffn_b2,
           L1_attn_wq, L1_attn_bq, L1_attn_wk, L1_attn_bk, L1_attn_wv,
           L1_attn_bv, L1_attn_wo, L1_attn_bo,
           L1_ffn_w_gate, L1_ffn_w_up, L1_ffn_w_down):
    x = np.ascontiguousarray(np.asarray(L0_attn_x, np.float32)[0])
    xs = shard_rows(x)

    gateWT = np.ascontiguousarray(np.asarray(L0_ffn_gate_W).T)
    b1T = np.ascontiguousarray(np.asarray(L0_ffn_b1).T)
    W1 = np.ascontiguousarray(np.asarray(L0_ffn_W1))
    W2 = np.ascontiguousarray(np.asarray(L0_ffn_W2))
    b2 = np.ascontiguousarray(np.asarray(L0_ffn_b2))
    Wqkv = np.ascontiguousarray(np.asarray(L0_attn_Wqkv))
    Wf1 = np.ascontiguousarray(np.concatenate(
        [np.asarray(L1_attn_wq), np.asarray(L1_attn_wk), np.asarray(L1_attn_wv)], 1))
    bf1 = np.ascontiguousarray(np.concatenate(
        [np.asarray(L1_attn_bq), np.asarray(L1_attn_bk), np.asarray(L1_attn_bv)]))

    xs = _moe_sublayer2(xs, gateWT, W1, b1T, W2, b2)
    xs = _attn_sublayer(xs, Wqkv, np.asarray(L0_attn_bqkv),
                        np.ascontiguousarray(np.asarray(L0_attn_Wo)),
                        np.asarray(L0_attn_bo), 2048, 2048, 16, True, "0")
    xs = _moe_sublayer2(xs, gateWT, W1, b1T, W2, b2)
    tern = _tern_prep2(np.asarray(L1_ffn_w_gate), np.asarray(L1_ffn_w_up),
                       np.asarray(L1_ffn_w_down))
    xs = _l1ffn_sublayer2(xs, tern)
    xs = _attn_sublayer(xs, Wf1, bf1,
                        np.ascontiguousarray(np.asarray(L1_attn_wo)),
                        np.asarray(L1_attn_bo), 2048, 512, 4, False, "1")
    xs = _l1ffn_sublayer2(xs, tern)

    out = unshard_rows(xs)
    return np.ascontiguousarray(out[None], dtype=np.float32)
